# version 06 (final)
# baseline (speedup 1.0000x reference)
import math
import os
from contextlib import ExitStack
_SKIP = set(os.environ.get("K_SKIP", "").split(","))
_C0SUB = int(os.environ.get("K_C0SUB", "9"))

import numpy as np
import ml_dtypes

import concourse.bass as bass
import concourse.mybir as mybir
from concourse.bass_utils import run_bass_kernel_spmd

F32 = mybir.dt.float32
BF16 = mybir.dt.bfloat16
AF = mybir.ActivationFunctionType
ALU = mybir.AluOpType
AX = mybir.AxisListType

NCORES = 8
B, T, D = 16, 2048, 1024
SPC = B // NCORES
NT = T // 128
NB = T // 512
H, DH, DR, DKV = 16, 64, 32, 256
NE, R = 6, 16
SCALE0 = 1.0 / 32.0
SCALE1 = 1.0 / math.sqrt(DH + DR)
NEGINF = -1e30
VS = H * 65
QR_PERM = [1, 3, 0, 2]
QR_ROW = {1: 0, 3: 32, 0: 64, 2: 96}


class Op:
    __slots__ = ("eng", "fn", "deps", "amt", "need", "val")

    def __init__(self, eng, fn, deps, amt):
        self.eng, self.fn, self.deps, self.amt = eng, fn, deps, amt
        self.need = False
        self.val = 0


class Sched:
    ENGS = ("t", "v", "s", "y", "g")

    def __init__(self, nc):
        self.nc = nc
        self.ops = {e: [] for e in self.ENGS}

    def op(self, eng, fn, deps=()):
        flat = []
        for d in deps:
            if d is None:
                continue
            if isinstance(d, (list, tuple)):
                flat.extend(x for x in d if x is not None)
            else:
                flat.append(d)
        deps = tuple(flat)
        amt = 16 if eng in ("y", "g") else 1
        o = Op(eng, fn, deps, amt)
        for d in deps:
            d.need = True
        self.ops[eng].append(o)
        return o

    def finalize(self, ctx):
        nc = self.nc
        sems = {e: ctx.enter_context(nc.semaphore(f"sem_{e}"))
                for e in self.ENGS}
        for e in ("y", "g"):
            for o in self.ops[e]:
                o.need = True
        for e in self.ENGS:
            cnt = 0
            for o in self.ops[e]:
                if o.need:
                    cnt += o.amt
                    o.val = cnt
        block = ctx.enter_context(nc.Block())
        engmap = {"t": block.tensor, "v": block.vector, "s": block.scalar,
                  "y": block.sync, "g": block.gpsimd}

        def make(e):
            def body(eng):
                waited = {}
                for o in self.ops[e]:
                    for d in o.deps:
                        if waited.get(d.eng, 0) < d.val:
                            eng.wait_ge(sems[d.eng], d.val)
                            waited[d.eng] = d.val
                    inst = o.fn(eng)
                    if o.need:
                        inst.then_inc(sems[o.eng], o.amt)
                if e in ("y", "g"):
                    tot = sum(o.amt for o in self.ops[e] if o.need)
                    if tot:
                        eng.wait_ge(sems[e], tot)
            return body
        for e in self.ENGS:
            engmap[e](make(e))


def sb(t, off, dims):
    return bass.AP(t, off, dims)


def build_nc(verbose=False, upto=99, debug=False):
    nc = bass.Bass("TRN2", target_bir_lowering=False)
    dt = nc.dram_tensor

    x_in = dt("x", [SPC * T, D], F32, kind="ExternalInput").ap()
    out_t = dt("out", [SPC * T, D], F32, kind="ExternalOutput").ap()
    xw = dt("xw", [SPC * T, D], F32).ap()
    wfin_hbm = dt("wfin_hbm", [128, 3 * 2048], BF16).ap()
    if debug:
        dbgw = dt("dbgw", [128, 3 * 2048], BF16, kind="ExternalOutput").ap()
        dbgx = dt("dbgx", [128, 8 * T], BF16, kind="ExternalOutput").ap()
        dbgk = dt("dbgk", [128, 8 * T], BF16, kind="ExternalOutput").ap()
        dbgv = dt("dbgv", [128, NT * VS], BF16, kind="ExternalOutput").ap()
        dbgr = dt("dbgr", [128, T], BF16, kind="ExternalOutput").ap()
        dwt3 = dt("dwt3", [32, 8192], F32, kind="ExternalOutput").ap()
        dwt4 = dt("dwt4", [32, 8192], F32, kind="ExternalOutput").ap()
        dwff = dt("dwff", [128, 2048], F32, kind="ExternalOutput").ap()
        dg1 = dt("dg1", [4, 2048], F32, kind="ExternalOutput").ap()
        dpy = dt("dpy", [32, 512], F32, kind="ExternalOutput").ap()
        dg1r = dt("dg1r", [4, 1024], F32, kind="ExternalOutput").ap()
        dln = dt("dln", [128, 4], F32, kind="ExternalOutput").ap()
        dbias = dt("dbias", [128, 512], F32, kind="ExternalOutput").ap()

    l0wq = dt("l0wq", [D, D], F32, kind="ExternalInput").ap()
    l0wk = dt("l0wk", [D, D], F32, kind="ExternalInput").ap()
    l0wv = dt("l0wv", [D, D], F32, kind="ExternalInput").ap()
    l0wo = dt("l0wo", [D, D], F32, kind="ExternalInput").ap()
    g1r_t = dt("g1r", [4, 1024], F32, kind="ExternalInput").ap()
    g2r_t = dt("g2r", [4, 256], F32, kind="ExternalInput").ap()
    g1i_t = dt("g1i", [4, 1024], F32, kind="ExternalInput").ap()
    g2i_t = dt("g2i", [4, 256], F32, kind="ExternalInput").ap()
    biasr_t = dt("biasr", [128, 512], F32, kind="ExternalInput").ap()
    biasi_t = dt("biasi", [128, 512], F32, kind="ExternalInput").ap()
    l1wq = dt("l1wq", [D, H * DH], F32, kind="ExternalInput").ap()
    l1wqr = dt("l1wqr", [D, H * DR], F32, kind="ExternalInput").ap()
    l1wkr = dt("l1wkr", [D, DR], F32, kind="ExternalInput").ap()
    l1wkvd = dt("l1wkvd", [D, DKV], F32, kind="ExternalInput").ap()
    l1wku = dt("l1wku", [DKV, H * DH], F32, kind="ExternalInput").ap()
    l1wvu = dt("l1wvu", [DKV, H * DH], F32, kind="ExternalInput").ap()
    l1wo = dt("l1wo", [D, D], F32, kind="ExternalInput").ap()
    lobo_t = dt("lobo", [128, H], F32, kind="ExternalInput").ap()
    basewt_t = dt("basewt", [D, D], F32, kind="ExternalInput").ap()
    basebbc_t = dt("basebbc", [128, D], F32, kind="ExternalInput").ap()
    sharedat_t = dt("sharedat", [D, R], F32, kind="ExternalInput").ap()
    sharedbt_t = dt("sharedbt", [R, D], F32, kind="ExternalInput").ap()
    routerwt_t = dt("routerwt", [D, NE], F32, kind="ExternalInput").ap()
    routerbbc_t = dt("routerbbc", [128, NE], F32, kind="ExternalInput").ap()
    routedat_t = dt("routedat", [D, NE * R], F32, kind="ExternalInput").ap()
    routedbt_t = dt("routedbt", [NE * R, D], F32, kind="ExternalInput").ap()
    identbf_t = dt("identbf", [128, 128], BF16, kind="ExternalInput").ap()
    identf_t = dt("identf", [128, 128], F32, kind="ExternalInput").ap()
    maskq_t = dt("maskq", [128, 128], F32, kind="ExternalInput").ap()
    maskk_t = dt("maskk", [128, 128], F32, kind="ExternalInput").ap()
    ropeqc_t = dt("ropeqc", [T, H * DR], F32, kind="ExternalInput").ap()
    ropeqs_t = dt("ropeqs", [T, H * DR], F32, kind="ExternalInput").ap()
    ropekc_t = dt("ropekc", [T, DR], F32, kind="ExternalInput").ap()
    ropeks_t = dt("ropeks", [T, DR], F32, kind="ExternalInput").ap()

    S = Sched(nc)
    ctx = ExitStack()
    E = ctx.enter_context

    identbf = E(nc.sbuf_tensor("identbf_sb", [128, 128], BF16))
    identf = E(nc.sbuf_tensor("identf_sb", [128, 128], F32))
    onesb = E(nc.sbuf_tensor("onesb_sb", [128, 1], BF16))
    onesf = E(nc.sbuf_tensor("onesf_sb", [1, 64], F32))
    ones11 = E(nc.sbuf_tensor("ones11_sb", [1, 1], F32))
    maskq = E(nc.sbuf_tensor("maskq_sb", [128, 128], F32))
    maskk = E(nc.sbuf_tensor("maskk_sb", [128, 128], F32))
    lobo_sb = E(nc.sbuf_tensor("lobo_sb", [128, H], F32))
    cexp_sb = E(nc.sbuf_tensor("cexp_sb", [128, H], F32))
    biasr_sb = E(nc.sbuf_tensor("biasr_sb", [128, 512], BF16))
    biasi_sb = E(nc.sbuf_tensor("biasi_sb", [128, 512], BF16))
    baseb_sb = E(nc.sbuf_tensor("baseb_sb", [128, D], BF16))
    routerb_sb = E(nc.sbuf_tensor("routerb_sb", [128, NE], F32))
    shat_sb = E(nc.sbuf_tensor("shat_sb", [128, 8 * R], BF16))
    shbt_sb = E(nc.sbuf_tensor("shbt_sb", [R, D], BF16))
    rwt_sb = E(nc.sbuf_tensor("rwt_sb", [128, 8 * NE], BF16))
    rat_sb = E(nc.sbuf_tensor("rat_sb", [128, 8 * NE * R], BF16))
    rbt_sb = E(nc.sbuf_tensor("rbt_sb", [NE * R, D], BF16))

    pq = E(nc.psum_tensor("pq", [128, 1024], F32))
    pk = E(nc.psum_tensor("pk", [128, 2048], F32))
    pY = E(nc.psum_tensor("pY", [128, 512], F32))
    pT = E(nc.psum_tensor("pT", [128, 1024], BF16))

    tld = []
    if "tldy" not in _SKIP:
        tld.append(S.op("y", lambda e: e.dma_start(identbf[:, :], identbf_t[:, :])))
        tld.append(S.op("y", lambda e: e.dma_start(identf[:, :], identf_t[:, :])))
        tld.append(S.op("y", lambda e: e.dma_start(maskq[:, :], maskq_t[:, :])))
        tld.append(S.op("y", lambda e: e.dma_start(maskk[:, :], maskk_t[:, :])))
        tld.append(S.op("y", lambda e: e.dma_start(lobo_sb[:, :], lobo_t[:, :])))
        tld.append(S.op("y", lambda e: e.dma_start(routerb_sb[:, :], routerbbc_t[:, :])))
    else:
        tld.append(S.op("g", lambda e: e.dma_start(lobo_sb[:, :], lobo_t[:, :])))
        tld = tld * 5
    tld.append(S.op("g", lambda e: e.dma_start(biasr_sb[:, :], biasr_t[:, :])))
    tld.append(S.op("g", lambda e: e.dma_start(biasi_sb[:, :], biasi_t[:, :])))
    tld.append(S.op("g", lambda e: e.dma_start(baseb_sb[:, :], basebbc_t[:, :])))
    tld.append(S.op("g", lambda e: e.dma_start(
        shat_sb[:, :], sharedat_t.rearrange("(c p) r -> p c r", p=128))))
    tld.append(S.op("g", lambda e: e.dma_start(shbt_sb[:, :], sharedbt_t[:, :])))
    tld.append(S.op("g", lambda e: e.dma_start(
        rwt_sb[:, :], routerwt_t.rearrange("(c p) r -> p c r", p=128))))
    tld.append(S.op("g", lambda e: e.dma_start(
        rat_sb[:, :], routedat_t.rearrange("(c p) r -> p c r", p=128))))
    tld.append(S.op("g", lambda e: e.dma_start(rbt_sb[:, :], routedbt_t[:, :])))
    if "memset" not in _SKIP:
        S.op("v", lambda e: e.memset(onesb[:, :], 1.0))
        eps_done = S.op("v", lambda e: e.memset(miscf[:, 84:85], 1e-5))
        S.op("v", lambda e: e.memset(onesf[:, :], 1.0))
        S.op("v", lambda e: e.memset(ones11[:, :], 1.0))
    else:
        eps_done = None
    cexp_done = None
    if "cexp" not in _SKIP:
        cexp_done = S.op("s", lambda e: e.activation(
            cexp_sb[:, :], lobo_sb[:, :], AF.Exp), deps=[tld[4]])

    with ExitStack() as uctx:
        g1_sb = uctx.enter_context(nc.sbuf_tensor("g1_sb", [4, 2048], F32))
        g2_sb = uctx.enter_context(nc.sbuf_tensor("g2_sb", [4, 512], F32))
        wt3 = uctx.enter_context(nc.sbuf_tensor("wt3_sb", [32, 8192], F32))
        wt4 = uctx.enter_context(nc.sbuf_tensor("wt4_sb", [32, 8192], F32))
        wff = uctx.enter_context(nc.sbuf_tensor("wff_sb", [128, 2048], F32))
        wfb = uctx.enter_context(nc.sbuf_tensor("wfb_sb", [128, 2048], BF16))

        lgs = [S.op("g", lambda e: e.dma_start(g1_sb[:, 0:1024], g1r_t[:, :])),
               S.op("g", lambda e: e.dma_start(g2_sb[:, 0:256], g2r_t[:, :])),
               S.op("g", lambda e: e.dma_start(g1_sb[:, 1024:2048], g1i_t[:, :])),
               S.op("g", lambda e: e.dma_start(g2_sb[:, 256:512], g2i_t[:, :]))]
        park_refs = []
        prev_sh = []
        for ri in range(0 if "unfold" in _SKIP else 2):
            g1s = g1_sb[:, ri * 1024:(ri + 1) * 1024]
            g2s = g2_sb[:, ri * 256:(ri + 1) * 256]
            ev_refs = []
            prev_ev = None
            if debug and ri == 0:
                S.op("g", lambda e: e.dma_start(dg1[:, 0:1024], g1_sb[:, 0:1024]),
                     deps=lgs)
            for b2 in range(16):
                mms = []
                for j in range(2):
                    b = 2 * b2 + j
                    mms.append(S.op(
                        "t",
                        lambda e, b=b, j=j, g1s=g1s, g2s=g2s: e.matmul(
                            pY[0:32, j * 256:(j + 1) * 256],
                            g1s[:, b * 32:(b + 1) * 32], g2s,
                            start=True, stop=True),
                        deps=lgs + [prev_ev] + park_refs))
                prev_ev = S.op(
                    "v",
                    lambda e, b2=b2: e.tensor_copy(
                        wt3[0:32, b2 * 512:(b2 + 1) * 512], pY[0:32, 0:512]),
                    deps=mms)
                ev_refs.append(prev_ev)
                if debug and ri == 0 and b2 == 0:
                    S.op("g", lambda e: e.dma_start(
                        dpy[:, :], wt3[0:32, 0:512]), deps=[prev_ev])
            ro_refs = []
            for g in range(16):
                src = sb(wt3, g * 16, [[8192, 32], [256, 32], [1, 16]])
                dst = sb(wt4, g * 512, [[8192, 32], [16, 32], [1, 16]])
                ro_refs.append(S.op(
                    "v", lambda e, src=src, dst=dst: e.tensor_copy(dst, src),
                    deps=ev_refs + prev_sh))
            sh_refs = []
            for jj in range(4):
                for dd in range(8):
                    p = jj * 8 + dd
                    src = wt4[p:p + 1, :].rearrange("p (g n) -> p g n", g=16)
                    dst = wff[dd * 16:(dd + 1) * 16, jj * 512:(jj + 1) * 512]
                    sh_refs.append(S.op(
                        "g", lambda e, src=src, dst=dst: e.dma_start(dst, src),
                        deps=ro_refs))
            cast = S.op("v", lambda e: e.tensor_copy(wfb[:, :], wff[:, :]),
                        deps=sh_refs)
            prev_sh = sh_refs
            if debug and ri == 0:
                S.op("g", lambda e: e.dma_start(dwt3[:, :], wt3[0:32, :]),
                     deps=[cast])
                S.op("g", lambda e: e.dma_start(dwt4[:, :], wt4[0:32, :]))
                S.op("g", lambda e: e.dma_start(dwff[:, :], wff[:, :]))
            park = S.op("g", lambda e, ri=ri: e.dma_start(
                wfin_hbm[:, ri * 2048:(ri + 1) * 2048], wfb[:, :]), deps=[cast])
            park_refs = [park]
            if ri == 1:
                ncast = S.op("v", lambda e: e.tensor_scalar(
                    wfb[:, :], wff[:, :], -1.0, None, ALU.mult), deps=[park])
                park_refs.append(S.op("g", lambda e: e.dma_start(
                    wfin_hbm[:, 2 * 2048:3 * 2048], wfb[:, :]), deps=[ncast]))
    unfold_done = park_refs

    if upto == 0:
        class _Dummy:
            pass
        xnT = KT = KRr = Vsb = wbufA = wbufB = None
    xnT = xnT if upto == 0 else E(nc.sbuf_tensor("xnT_sb", [128, 8 * T], BF16))
    if upto != 0:
        KT = E(nc.sbuf_tensor("KT_sb", [128, 8 * T], BF16))
        KRr = E(nc.sbuf_tensor("KRr_sb", [128, T], BF16))
        Vsb = E(nc.sbuf_tensor("V_sb", [128, NT * VS], BF16))
        wbufA = E(nc.sbuf_tensor("wbufA_sb", [128, 8192], BF16))
        wbufB = E(nc.sbuf_tensor("wbufB_sb", [128, 12288], BF16))
    miscf = E(nc.sbuf_tensor("miscf_sb", [128, 1024], F32))
    miscb = E(nc.sbuf_tensor("miscb_sb", [128, 1024], BF16))
    xnbf = E(nc.sbuf_tensor("xnbf_sb", [128, 1024], BF16))
    sq_holder = {}
    nrmrow_t = E(nc.sbuf_tensor("nrmrow_sb", [1, 512], F32))
    nrmbc_sb = E(nc.sbuf_tensor("nrmbc_sb", [64, 512], F32))

    bn6 = [miscf[:, 0:12], miscf[:, 12:24]]
    stat2 = [miscf[:, 24:26], miscf[:, 26:28]]
    rr1 = [miscf[:, 28:29], miscf[:, 29:30]]
    nb1 = [miscf[:, 30:31], miscf[:, 31:32]]
    lgv = miscf[:, 32:38]
    gmax = miscf[:, 49:50]
    ggv = miscf[:, 40:46]
    m1n = miscf[:, 46:47]
    gsum = miscf[:, 47:48]
    mm2v = miscf[:, 48:49]
    w6 = miscf[:, 56:62]
    mq4 = miscf[:, 64:68]
    tm4 = miscf[:, 68:72]
    emq4 = miscf[:, 72:76]
    nrq4 = miscf[:, 76:80]
    nrmcol = miscf[:, 80:84]
    ropet = miscf[:, 128:640]
    ropet2 = miscf[:, 640:1024]
    denrow = miscf[0:1, 512:1024]
    h_bf = miscb[:, 0:96]
    hT_bf = miscb[0:96, 128:256]
    s1_bf = miscb[0:16, 256:384]
    krope_bf = miscb[:, 384:416]
    qrope_bf = miscb[:, 512:1024]
    ystage = miscb[0:64, 512:1024]


    wstate = {"A": {"last": list(unfold_done)}, "B": {"last": []}}

    def load_w(buf, dram, ncols, dst_off=0):
        tgt = wbufA if buf == "A" else wbufB
        st = wstate[buf]
        nch = dram.shape[0] // 128
        r = S.op("g", lambda e: e.dma_start(
            tgt[:, dst_off:dst_off + nch * ncols],
            dram.rearrange("(c p) n -> p c n", p=128)),
            deps=st["last"] + tld)
        st["last"] = []
        return r

    def use_w(buf, op):
        wstate[buf]["last"].append(op)

    last_store = [None] * (SPC * NT)
    mark = {"xnT": [], "KTV": [], "C_T": [], "A_V": [], "C_V": []}

    def gslot(s, i):
        return s * NT + i

    lnstash = {"want": True}

    def layernorm(par, xs, dep_load, extra):
        sq_sb = sq_holder["t"]
        sm = bn6[par][:, 0:1]
        sq = bn6[par][:, 1:2]
        tmp = bn6[par][:, 2:3]
        var = bn6[par][:, 3:4]
        sqo = S.op("v", lambda e: e.tensor_tensor(
            sq_sb[:, :], xs, xs, ALU.mult), deps=[dep_load] + extra)
        o1 = S.op("v", lambda e: e.reduce_sum(sm, xs, axis=AX.X))
        o2 = S.op("v", lambda e: e.reduce_sum(sq, sq_sb[:, :], axis=AX.X),
                  deps=[sqo])
        o3 = S.op("v", lambda e: e.tensor_scalar(
            sm, sm, 1.0 / D, None, ALU.mult), deps=[o1])
        o4 = S.op("v", lambda e: e.tensor_tensor(tmp, sm, sm, ALU.mult),
                  deps=[o3])
        o5 = S.op("v", lambda e: e.tensor_scalar(
            var, sq, 1.0 / D, None, ALU.mult), deps=[o2])
        b3 = S.op("v", lambda e: e.tensor_tensor(
            var, var, tmp, ALU.subtract), deps=[o4, o5])
        l1_ = S.op("s", lambda e: e.activation(
            rr1[par], var, AF.Ln, bias=miscf[:, 84:85]),
            deps=[b3, eps_done])
        l2_ = S.op("s", lambda e: e.activation(
            rr1[par], rr1[par], AF.Exp, scale=-0.5), deps=[l1_])
        cast = S.op("v", lambda e: e.tensor_scalar(
            xnbf[:, :], xs, sm, rr1[par],
            ALU.subtract, ALU.mult), deps=[l2_, o3])
        if lnstash is not None and lnstash.get("want"):
            lnstash["want"] = False
            S.op("v", lambda e: e.tensor_copy(miscf[:, 100:101], sm), deps=[cast])
            S.op("v", lambda e: e.tensor_copy(miscf[:, 101:102], var))
            S.op("v", lambda e: e.tensor_copy(miscf[:, 102:103], rr1[par]))
        return cast

    def emit_A(s, layer, x_sb, w_ref):
        xsrc = x_in if layer == 0 else xw
        xnT_readers = list(mark["C_T"]) + list(mark["C_V"])
        loads = [None] * NT
        loads[0] = S.op("y", lambda e: e.dma_start(
            sb(x_sb, 0, [[2 * D, 128], [1, D]]),
            xsrc[gslot(s, 0) * 128:(gslot(s, 0) + 1) * 128, :]),
            deps=[last_store[gslot(s, 0)]])
        tile_consumed = [None, None]
        store_par = [None, None]
        prev_evac = None
        prev_ffn = None
        prev_cast = None
        for i in range(NT):
            par = i % 2
            slot = gslot(s, i)
            xs = sb(x_sb, par * D, [[2 * D, 128], [1, D]])
            if i + 1 < NT:
                nslot = gslot(s, i + 1)
                loads[i + 1] = S.op("y", lambda e, i=i, nslot=nslot: e.dma_start(
                    sb(x_sb, ((i + 1) % 2) * D, [[2 * D, 128], [1, D]]),
                    xsrc[nslot * 128:(nslot + 1) * 128, :]),
                    deps=[tile_consumed[(i + 1) % 2], last_store[nslot],
                          store_par[(i + 1) % 2]])
            cast = layernorm(par, xs, loads[i],
                             [prev_cast, prev_evac] if prev_cast else [])
            prev_cast = cast
            trs = []
            for c in range(8):
                trs.append(S.op("t", lambda e, c=c: e.transpose(
                    pT[:, c * 128:(c + 1) * 128],
                    xnbf[:, c * 128:(c + 1) * 128], identbf[:, :]),
                    deps=[cast, prev_evac,
                          prev_ffn[1] if prev_ffn else None] +
                         (xnT_readers if i == 0 else []) if c == 0 else [cast]))
            for c in range(7):
                S.op("v", lambda e, c=c, i=i: e.tensor_copy(
                    xnT[:, c * T + i * 128:c * T + i * 128 + 128],
                    pT[:, c * 128:(c + 1) * 128]), deps=trs if c == 0 else [])
            evac = S.op("v", lambda e, i=i: e.tensor_copy(
                xnT[:, 7 * T + i * 128:7 * T + i * 128 + 128],
                pT[:, 7 * 128:8 * 128]))
            prev_evac = evac
            if layer == 0:
                prev_ffn = emit_ffn0(s, i, xs, evac, loads[i], prev_ffn, w_ref,
                                     cast)
            else:
                prev_ffn = emit_ffn1(s, i, xs, evac, loads[i], prev_ffn, w_ref,
                                     cast)
            st = S.op("y", lambda e, slot=slot, xs=xs: e.dma_start(
                xw[slot * 128:(slot + 1) * 128, :], xs), deps=[prev_ffn[0]])
            last_store[slot] = st
            store_par[par] = st
            tile_consumed[par] = prev_ffn[0]
        mark["xnT"] = [prev_evac]
        mark["A_V"] = [prev_ffn[0]]

    def emit_ffn0(s, i, xs, evac, xload, prev_ffn, w_ref, cast):
        xnt_i = lambda c: sb(xnT, c * T + i * 128, [[8 * T, 128], [1, 128]])
        mms = []
        for half in range(2):
            for c in range(4):
                wr_off = (0 if half == 0 else 2048) + c * 512
                mms.append(S.op("t", lambda e, c=c, half=half, wr_off=wr_off,
                                xnt_i=xnt_i: e.matmul(
                                    pk[:, half * 512:(half + 1) * 512],
                                    xnt_i(c), wbufB[:, wr_off:wr_off + 512],
                                    start=(c == 0), stop=False),
                                deps=[evac, w_ref, prev_ffn[0] if prev_ffn else None]
                                if c == 0 else []))
            for c in range(4):
                wi_off = (4096 if half == 0 else 0) + c * 512
                mms.append(S.op("t", lambda e, c=c, half=half, wi_off=wi_off,
                                xnt_i=xnt_i: e.matmul(
                                    pk[:, half * 512:(half + 1) * 512],
                                    xnt_i(c + 4), wbufB[:, wi_off:wi_off + 512],
                                    start=False, stop=(c == 3))))
        use_w("B", mms[-1])
        S.op("v", lambda e, xs=xs: e.tensor_tensor(
            xs[:, 0:512], xs[:, 0:512], biasr_sb[:, :], ALU.add),
            deps=[xload, cast])
        S.op("v", lambda e, xs=xs: e.tensor_tensor(
            xs[:, 512:1024], xs[:, 512:1024], biasi_sb[:, :], ALU.add))
        S.op("v", lambda e, xs=xs: e.tensor_tensor(
            xs[:, 0:512], xs[:, 0:512], pk[:, 0:512], ALU.add), deps=[mms[-1]])
        a4 = S.op("v", lambda e, xs=xs: e.tensor_tensor(
            xs[:, 512:1024], xs[:, 512:1024], pk[:, 512:1024], ALU.add))
        return (a4, None)

    def emit_ffn1(s, i, xs, evac, xload, prev_ffn, w_ref, cast):
        xnt_i = lambda c: sb(xnT, c * T + i * 128, [[8 * T, 128], [1, 128]])
        main_mms, lg_mms, h_mms, s1_mms = [], [], [], []
        for c in range(8):
            for g in range(2):
                main_mms.append(S.op(
                    "t", lambda e, c=c, g=g, xnt_i=xnt_i: e.matmul(
                        pk[:, g * 512:(g + 1) * 512], xnt_i(c),
                        wbufA[:, c * 1024 + g * 512:c * 1024 + (g + 1) * 512],
                        start=(c == 0), stop=False),
                    deps=[evac, w_ref, prev_ffn[0] if prev_ffn else None]
                    if c == 0 and g == 0 else []))
        for c in range(8):
            lg_mms.append(S.op("t", lambda e, c=c, xnt_i=xnt_i: e.matmul(
                pq[:, 0:NE], xnt_i(c), rwt_sb[:, c * NE:(c + 1) * NE],
                start=(c == 0), stop=(c == 7))))
        for c in range(8):
            h_mms.append(S.op("t", lambda e, c=c, xnt_i=xnt_i: e.matmul(
                pq[:, 128:224], xnt_i(c), rat_sb[:, c * 96:(c + 1) * 96],
                start=(c == 0), stop=(c == 7))))
        for c in range(8):
            s1_mms.append(S.op("t", lambda e, c=c, xnt_i=xnt_i: e.matmul(
                pq[0:16, 256:384], shat_sb[:, c * R:(c + 1) * R], xnt_i(c),
                start=(c == 0), stop=(c == 7))))
        use_w("A", main_mms[-1])
        s1ev = S.op("v", lambda e: e.tensor_copy(s1_bf[:, :], pq[0:16, 256:384]),
                    deps=[s1_mms[-1]])
        sh2 = []
        for g in range(2):
            sh2.append(S.op("t", lambda e, g=g: e.matmul(
                pk[:, g * 512:(g + 1) * 512], s1_bf[:, :],
                shbt_sb[:, g * 512:(g + 1) * 512], start=False, stop=False),
                deps=[s1ev] if g == 0 else []))
        r1 = S.op("v", lambda e: e.tensor_tensor(
            lgv, pq[:, 0:NE], routerb_sb[:, :], ALU.add),
            deps=[lg_mms[-1], s1_mms[-1]])
        r2 = S.op("v", lambda e: e.reduce_max(m1n, lgv, axis=AX.X, negate=True),
                  deps=[r1])
        r3 = S.op("s", lambda e: e.activation(
            ggv, lgv, AF.Exp, bias=m1n), deps=[r2])
        r3b = S.op("v", lambda e: e.reduce_sum(gsum, ggv, axis=AX.X), deps=[r3])
        r3c = S.op("v", lambda e: e.reduce_max(gmax, ggv, axis=AX.X),
                   deps=[r3b])
        r4 = S.op("v", lambda e: e.tensor_single_scalar(
            w6, ggv, gmax, ALU.is_ge), deps=[r3c])
        r5 = S.op("v", lambda e: e.scalar_tensor_tensor(
            w6, w6, -2.0, ggv, ALU.mult, ALU.add), deps=[r4])
        r6 = S.op("v", lambda e: e.reduce_max(mm2v, w6, axis=AX.X), deps=[r5])
        r7 = S.op("v", lambda e: e.tensor_single_scalar(w6, ggv, mm2v, ALU.is_ge),
                  deps=[r6])
        r8 = S.op("v", lambda e: e.tensor_tensor(w6, w6, ggv, ALU.mult), deps=[r7])
        r9 = S.op("v", lambda e: e.reduce_sum(mm2v, w6, axis=AX.X), deps=[r8])
        r10 = S.op("v", lambda e: e.scalar_tensor_tensor(
            mm2v, gsum, 1e-6, mm2v, ALU.mult, ALU.add), deps=[r9, r3b])
        r11 = S.op("v", lambda e: e.reciprocal(m1n, mm2v), deps=[r10])
        r12 = S.op("v", lambda e: e.tensor_scalar(w6, w6, m1n, None, ALU.mult),
                   deps=[r11])
        hw = []
        for ex in range(NE):
            hw.append(S.op("v", lambda e, ex=ex: e.tensor_scalar(
                h_bf[:, ex * R:(ex + 1) * R],
                pq[:, 128 + ex * R:128 + (ex + 1) * R],
                w6[:, ex:ex + 1], None, ALU.mult), deps=[r12, h_mms[-1]]))
        htr = S.op("t", lambda e: e.transpose(
            pT[0:96, 0:128], h_bf[:, :], identbf[:, :]), deps=hw + [evac])
        hev = S.op("v", lambda e: e.tensor_copy(hT_bf[:, :], pT[0:96, 0:128]),
                   deps=[htr])
        rb = []
        for g in range(2):
            rb.append(S.op("t", lambda e, g=g: e.matmul(
                pk[:, g * 512:(g + 1) * 512], hT_bf[:, :],
                rbt_sb[:, g * 512:(g + 1) * 512],
                start=False, stop=True), deps=[hev] if g == 0 else []))
        S.op("v", lambda e, xs=xs: e.tensor_tensor(
            xs, xs, baseb_sb[:, :], ALU.add), deps=[xload, cast])
        S.op("v", lambda e, xs=xs: e.tensor_tensor(
            xs[:, 0:512], xs[:, 0:512], pk[:, 0:512], ALU.add),
            deps=rb + sh2 + [main_mms[-1]])
        a3 = S.op("v", lambda e, xs=xs: e.tensor_tensor(
            xs[:, 512:1024], xs[:, 512:1024], pk[:, 512:1024], ALU.add))
        return (a3, hev)

    def emit_B0(s):
        wk = load_w("A", l0wk, 1024)
        xde = mark["xnT"]
        av = mark["A_V"]
        ring = [None, None, None, None]
        last_mm = None
        for ck in range(8):
            for g in range(4):
                slot_ = (ck * 4 + g) % 4
                mms = []
                for dc in range(8):
                    mms.append(S.op("t", lambda e, ck=ck, g=g, dc=dc, slot_=slot_:
                                    e.matmul(
                                        pk[:, slot_ * 512:(slot_ + 1) * 512],
                                        wbufA[:, dc * 1024 + ck * 128:
                                              dc * 1024 + (ck + 1) * 128],
                                        sb(xnT, dc * T + g * 512,
                                           [[8 * T, 128], [1, 512]]),
                                        start=(dc == 0), stop=(dc == 7)),
                                    deps=[wk, ring[slot_]] + (xde + av
                                    if ck == 0 and g == 0 else [])
                                    if dc == 0 else []))
                ring[slot_] = S.op("v", lambda e, ck=ck, g=g, slot_=slot_:
                                   e.tensor_copy(
                                       KT[:, ck * T + g * 512:ck * T + (g + 1) * 512],
                                       pk[:, slot_ * 512:(slot_ + 1) * 512]),
                                   deps=[mms[-1]])
                last_mm = mms[-1]
        use_w("A", last_mm)
        wv = load_w("A", l0wv, 1024)
        prev = None
        for i in range(NT):
            mms = []
            for dc in range(8):
                for g in range(2):
                    mms.append(S.op("t", lambda e, i=i, dc=dc, g=g: e.matmul(
                        pq[:, g * 512:(g + 1) * 512],
                        sb(xnT, dc * T + i * 128, [[8 * T, 128], [1, 128]]),
                        wbufA[:, dc * 1024 + g * 512:dc * 1024 + (g + 1) * 512],
                        start=(dc == 0), stop=(dc == 7)),
                        deps=[wv, prev, av[0]] if dc == 0 and g == 0 else []))
            prev = S.op("v", lambda e, i=i: e.tensor_copy(
                Vsb[:, i * VS:i * VS + 1024], pq[:, 0:1024]), deps=[mms[-1]])
        use_w("A", mms[-1])
        mark["KTV"] = list(ring) + [prev]

    def emit_B1(s):
        xde = mark["xnT"]
        av = mark["A_V"]
        with ExitStack() as bctx:
            latT = bctx.enter_context(nc.sbuf_tensor(f"latT_sb_{s}", [128, 2 * T], BF16))
            wkvd = load_w("A", l1wkvd, DKV, dst_off=0)
            wkr = load_w("A", l1wkr, DR, dst_off=2048)
            lat_ev = []
            last_mm = None
            for ck in range(2):
                for g in range(4):
                    mms = []
                    for dc in range(8):
                        mms.append(S.op("t", lambda e, ck=ck, g=g, dc=dc: e.matmul(
                            pk[:, (g % 2) * 512:(g % 2 + 1) * 512],
                            wbufA[:, dc * 256 + ck * 128:dc * 256 + (ck + 1) * 128],
                            sb(xnT, dc * T + g * 512, [[8 * T, 128], [1, 512]]),
                            start=(dc == 0), stop=(dc == 7)),
                            deps=[wkvd,
                                  lat_ev[-2] if len(lat_ev) >= 2 else None] +
                                 (xde + av if ck == 0 and g == 0 else [])
                            if dc == 0 else []))
                    lat_ev.append(S.op("v", lambda e, ck=ck, g=g: e.tensor_copy(
                        latT[:, ck * T + g * 512:ck * T + (g + 1) * 512],
                        pk[:, (g % 2) * 512:(g % 2 + 1) * 512]), deps=[mms[-1]]))
                    last_mm = mms[-1]
            krev = []
            prev_tr = None
            prev_rope = None
            for i in range(NT):
                mms = []
                for dc in range(8):
                    mms.append(S.op("t", lambda e, i=i, dc=dc: e.matmul(
                        pq[:, 0:DR],
                        sb(xnT, dc * T + i * 128, [[8 * T, 128], [1, 128]]),
                        wbufA[:, 2048 + dc * DR:2048 + (dc + 1) * DR],
                        start=(dc == 0), stop=(dc == 7)),
                        deps=[wkr, prev_rope, av[0]] if dc == 0 else []))
                t1 = S.op("y", lambda e, i=i: e.dma_start(
                    ropet2[:, 0:32], ropekc_t[i * 128:(i + 1) * 128, :]),
                    deps=[prev_rope])
                t2 = S.op("y", lambda e, i=i: e.dma_start(
                    ropet2[:, 32:64], ropeks_t[i * 128:(i + 1) * 128, :]))
                k1 = S.op("v", lambda e: e.tensor_tensor(
                    ropet2[:, 64:96], pq[:, 0:32], ropet2[:, 0:32], ALU.mult),
                    deps=[mms[-1], t1, t2])
                k2 = S.op("v", lambda e: e.tensor_tensor(
                    ropet2[:, 96:112], pq[:, 16:32], ropet2[:, 32:48], ALU.mult))
                k3 = S.op("v", lambda e: e.tensor_tensor(
                    ropet2[:, 112:128], pq[:, 0:16], ropet2[:, 48:64], ALU.mult))
                prev_rope = S.op("v", lambda e: e.tensor_tensor(
                    krope_bf[:, :], ropet2[:, 64:96], ropet2[:, 96:128], ALU.add),
                    deps=[k1, k2, k3])
                trs = []
                for q in range(4):
                    trs.append(S.op("t", lambda e, q=q: e.transpose(
                        pT[q * 32:(q + 1) * 32, 0:128], krope_bf[:, :],
                        identbf[:, :], tile_position=(0, q * 32)),
                        deps=[prev_rope, prev_tr] if q == 0 else []))
                prev_tr = S.op("v", lambda e, i=i: e.tensor_copy(
                    KRr[:, i * 128:(i + 1) * 128], pT[:, 0:128]), deps=trs)
                krev.append(prev_tr)
            use_w("A", mms[-1])
            use_w("A", last_mm)
            wku = load_w("A", l1wku, H * DH, dst_off=0)
            kc_ev = []
            last_mm = None
            for j in range(8):
                for g in range(4):
                    mms = []
                    for dc in range(2):
                        mms.append(S.op("t", lambda e, j=j, g=g, dc=dc: e.matmul(
                            pk[:, (g % 2) * 512:(g % 2 + 1) * 512],
                            wbufA[:, dc * 1024 + j * 128:dc * 1024 + (j + 1) * 128],
                            sb(latT, dc * T + g * 512, [[2 * T, 128], [1, 512]]),
                            start=(dc == 0), stop=(dc == 1)),
                            deps=[wku, lat_ev[-1],
                                  kc_ev[-2] if len(kc_ev) >= 2 else None]
                            if dc == 0 else []))
                    kc_ev.append(S.op("v", lambda e, j=j, g=g: e.tensor_copy(
                        KT[:, j * T + g * 512:j * T + (g + 1) * 512],
                        pk[:, (g % 2) * 512:(g % 2 + 1) * 512]), deps=[mms[-1]]))
                    last_mm = mms[-1]
            use_w("A", last_mm)
            wvu = load_w("A", l1wvu, H * DH, dst_off=2048)
            vev = []
            prev = None
            for i in range(NT):
                mms = []
                for dc in range(2):
                    for g in range(2):
                        mms.append(S.op("t", lambda e, i=i, dc=dc, g=g: e.matmul(
                            pq[:, g * 512:(g + 1) * 512],
                            sb(latT, dc * T + i * 128, [[2 * T, 128], [1, 128]]),
                            wbufA[:, 2048 + dc * 1024 + g * 512:
                                  2048 + dc * 1024 + (g + 1) * 512],
                            start=(dc == 0), stop=(dc == 1)),
                            deps=[wvu, lat_ev[-1], prev, krev[-1]]
                            if dc == 0 and g == 0 else []))
                c1 = S.op("v", lambda e, i=i: e.tensor_copy(
                    sb(Vsb, i * VS, [[NT * VS, 128], [65, 16], [1, 64]]),
                    sb(pq, 0, [[1024, 128], [64, 16], [1, 64]])),
                    deps=[mms[-1]])
                prev = S.op("v", lambda e, i=i: e.memset(
                    sb(Vsb, i * VS + 64, [[NT * VS, 128], [65, 16], [1, 1]]), 1.0))
                vev.append(prev)
            use_w("A", mms[-1])
            mark["KTV"] = [kc_ev[-1], kc_ev[-2], krev[-1], vev[-1], lat_ev[-1]]

    def emit_C0(s, x_sb):
        wq = load_w("A", l0wq, 1024)
        wo = load_w("B", l0wo, 1024, dst_off=0)
        with ExitStack() as cctx:
            Qblk = cctx.enter_context(nc.sbuf_tensor(f"Qblk_sb_{s}", [128, 8 * 512], BF16))
            PTf = cctx.enter_context(nc.sbuf_tensor(f"PTf_sb_{s}", [128, 16 * 512], BF16))
            Ysb = cctx.enter_context(nc.sbuf_tensor(f"Y0_sb_{s}", [128, 8 * 512], BF16))
            ktv = mark["KTV"]
            pb = {"q": None, "pv": None, "wo": None, "nrc": None, "ptf": None}
            cv_marks = []
            st_par = [None, None]
            for bi in range(NB):
                ncnk = 4 * (bi + 1)
                qev = []
                last_mm = None
                for j in range(8):
                    slot_ = j % 2
                    mms = []
                    for dc in range(8):
                        mms.append(S.op("t", lambda e, j=j, dc=dc, bi=bi, slot_=slot_:
                                        e.matmul(
                                            pk[:, slot_ * 512:(slot_ + 1) * 512],
                                            wbufA[:, dc * 1024 + j * 128:
                                                  dc * 1024 + (j + 1) * 128],
                                            sb(xnT, dc * T + bi * 512,
                                               [[8 * T, 128], [1, 512]]),
                                            start=(dc == 0), stop=(dc == 7)),
                                        deps=[wq, qev[-2] if len(qev) >= 2 else pb["pv"]] +
                                             (ktv + mark["A_V"] if bi == 0 and j == 0 else [])
                                        if dc == 0 else []))
                    qev.append(S.op("v", lambda e, j=j, slot_=slot_: e.tensor_scalar(
                        Qblk[:, j * 512:(j + 1) * 512],
                        pk[:, slot_ * 512:(slot_ + 1) * 512],
                        SCALE0, None, ALU.mult), deps=[mms[-1]]))
                    last_mm = mms[-1]
                use_w("A", last_mm)
                if _C0SUB < 2:
                    pb = {"q": qev[-1], "pv": last_mm, "wo": last_mm,
                          "nrc": qev[-1], "ptf": last_mm}
                    continue
                exp_refs = []
                den_last = None
                pair_reader = [pb["pv"], pb["pv"]]
                for c in range(ncnk):
                    ps_ = c % 2
                    q0 = max(0, c * 128 - bi * 512)
                    mms = []
                    for dc in range(8):
                        mms.append(S.op("t", lambda e, c=c, dc=dc, q0=q0, ps_=ps_:
                                        e.matmul(
                                            pk[:, ps_ * 1024 + q0:ps_ * 1024 + 512],
                                            KT[:, dc * T + c * 128:dc * T + (c + 1) * 128],
                                            Qblk[:, dc * 512 + q0:(dc + 1) * 512],
                                            start=(dc == 0), stop=(dc == 7)),
                                        deps=[qev[-1], pair_reader[ps_]]
                                        if dc == 0 else []))
                    dep_mask = [mms[-1]]
                    if c >= 4 * bi:
                        dep_mask = [S.op("v", lambda e, q0=q0, ps_=ps_: e.tensor_tensor(
                            pk[:, ps_ * 1024 + q0:ps_ * 1024 + q0 + 128],
                            pk[:, ps_ * 1024 + q0:ps_ * 1024 + q0 + 128],
                            maskk[:, :], ALU.add), deps=[mms[-1]])]
                    ex = S.op("s", lambda e, c=c, q0=q0, ps_=ps_: e.activation(
                        PTf[:, c * 512 + q0:(c + 1) * 512],
                        pk[:, ps_ * 1024 + q0:ps_ * 1024 + 512], AF.Exp),
                        deps=dep_mask + ([pb["ptf"]] if c == 0 else []))
                    exp_refs.append(ex)
                    pair_reader[ps_] = ex
                if _C0SUB < 3:
                    pb = {"q": qev[-1], "pv": exp_refs[-1], "wo": exp_refs[-1],
                          "nrc": exp_refs[-1], "ptf": exp_refs[-1]}
                    continue
                pv_last = None
                yev = []
                for dv in range(8):
                    for c in range(ncnk):
                        q0 = max(0, c * 128 - bi * 512)
                        pv_last = S.op("t", lambda e, c=c, dv=dv, q0=q0, ncnk=ncnk:
                                       e.matmul(
                                           pY[:, q0:512],
                                           Vsb[:, c * VS + dv * 128:c * VS + (dv + 1) * 128],
                                           PTf[:, c * 512 + q0:(c + 1) * 512],
                                           start=(c == 0), stop=(c == ncnk - 1)),
                                       deps=[exp_refs[-1],
                                             yev[-1] if yev else None]
                                       if c == 0 else [])
                    yev.append(S.op("v", lambda e, dv=dv: e.tensor_copy(
                        Ysb[:, dv * 512:(dv + 1) * 512], pY[:, :]), deps=[pv_last]))
                if _C0SUB < 4:
                    pb = {"q": qev[-1], "pv": pv_last, "wo": exp_refs[-1],
                          "nrc": yev[-1], "ptf": pv_last}
                    continue
                nmm = []
                for st_ in range(4):
                    last_c = 4 * bi + st_
                    for c in range(last_c + 1):
                        nmm.append(S.op("t", lambda e, c=c, st_=st_, last_c=last_c:
                                        e.matmul(
                                            pk[:, 1536 + st_:1537 + st_],
                                            PTf[:, c * 512 + st_ * 128:
                                                c * 512 + (st_ + 1) * 128],
                                            onesb[:, :],
                                            start=(c == 0), stop=(c == last_c)),
                                        deps=[exp_refs[-1], yev[-1], pb["wo"],
                                              pb["nrc"]]
                                        if st_ == 0 and c == 0 else []))
                cpy = S.op("v", lambda e: e.tensor_copy(
                    tm4[:, 0:4], pk[:, 1536:1540]), deps=[nmm[-1]])
                rec = S.op("v", lambda e: e.reciprocal(
                    nrmcol[:, 0:4], tm4[:, 0:4]), deps=[cpy])
                nrc = [rec, rec, rec, rec]
                if _C0SUB < 5:
                    pb = {"q": qev[-1], "pv": pv_last, "wo": nmm[-1],
                          "nrc": rec, "ptf": nmm[-1]}
                    continue
                wo_last = pb["wo"]
                for st_ in range(4):
                    slot_g = gslot(s, bi * 4 + st_)
                    xap = sb(x_sb, (st_ % 2) * D, [[2 * D, 128], [1, D]])
                    xl = S.op("y", lambda e, slot_g=slot_g, xap=xap: e.dma_start(
                        xap, xw[slot_g * 128:(slot_g + 1) * 128, :]),
                        deps=[last_store[slot_g], st_par[st_ % 2],
                              cv_marks[-2] if len(cv_marks) >= 2 else None])
                    mms = []
                    for dv in range(8):
                        for g in range(2):
                            mms.append(S.op("t", lambda e, st_=st_, dv=dv, g=g:
                                            e.matmul(
                                                pq[:, g * 512:(g + 1) * 512],
                                                Ysb[:, dv * 512 + st_ * 128:
                                                    dv * 512 + (st_ + 1) * 128],
                                                wbufB[:, dv * 1024 + g * 512:
                                                      dv * 1024 + (g + 1) * 512],
                                                start=(dv == 0), stop=(dv == 7)),
                                            deps=[wo, yev[-1], rec, wo_last,
                                                  cv_marks[-1] if cv_marks
                                                  else None]
                                            if dv == 0 and g == 0 else []))
                    wo_last = mms[-1]
                    add = S.op("v", lambda e, st_=st_, xap=xap: e.scalar_tensor_tensor(
                        xap, pq[:, 0:1024], nrmcol[:, st_:st_ + 1], xap,
                        ALU.mult, ALU.add), deps=[xl, wo_last, nrc[st_]])
                    cv_marks.append(add)
                    last_store[slot_g] = st_par[st_ % 2] = S.op(
                        "y", lambda e, slot_g=slot_g, xap=xap: e.dma_start(
                            xw[slot_g * 128:(slot_g + 1) * 128, :], xap),
                        deps=[add])
                use_w("B", wo_last)
                pb = {"q": qev[-1], "pv": pv_last, "wo": wo_last, "nrc": rec,
                      "ptf": nmm[-1]}
            mark["C_T"] = [S.ops["t"][-1]]
            mark["C_V"] = [cv_marks[-1]] if cv_marks else []

    def emit_C1(s, x_sb, final):
        wq = load_w("A", l1wq, H * DH)
        wqr = load_w("B", l1wqr, H * DR, dst_off=0)
        wo = load_w("B", l1wo, 1024, dst_off=4096)
        xdst = out_t if final else xw
        with ExitStack() as cctx:
            Qcb = cctx.enter_context(nc.sbuf_tensor(f"Qcb_sb_{s}", [128, 8 * 512], BF16))
            Qrb = cctx.enter_context(nc.sbuf_tensor(f"Qrb_sb_{s}", [128, 4 * 512], BF16))
            PTr = cctx.enter_context(nc.sbuf_tensor(f"PTr_sb_{s}", [128, 2048], BF16))
            Ysb = cctx.enter_context(nc.sbuf_tensor(f"Y1_sb_{s}", [128, 8 * 512], BF16))
            rqc = cctx.enter_context(nc.sbuf_tensor(f"rqc_sb_{s}", [128, 1024], F32))
            rqs = cctx.enter_context(nc.sbuf_tensor(f"rqs_sb_{s}", [128, 1024], F32))
            ktv = mark["KTV"]
            pv = {"qc": None, "qr": None, "pv": None, "wo": None, "yd": None,
                  "yd2": None, "qm": None, "nr": None, "em": None,
                  "e0": None, "e1": None, "cv": None}
            cv_marks = []
            st_par = [None, None]
            for bi in range(NB):
                ncnk = 4 * (bi + 1)
                qcev = []
                last_mm = None
                for j in range(8):
                    slot_ = j % 2
                    mms = []
                    for dc in range(8):
                        mms.append(S.op("t", lambda e, j=j, dc=dc, bi=bi, slot_=slot_:
                                        e.matmul(
                                            pk[:, slot_ * 512:(slot_ + 1) * 512],
                                            wbufA[:, dc * 1024 + j * 128:
                                                  dc * 1024 + (j + 1) * 128],
                                            sb(xnT, dc * T + bi * 512,
                                               [[8 * T, 128], [1, 512]]),
                                            start=(dc == 0), stop=(dc == 7)),
                                        deps=[wq, pv["yd"],
                                              qcev[-2] if len(qcev) >= 2 else pv["qc"]] +
                                             (ktv + mark["A_V"] if bi == 0 and j == 0 else [])
                                        if dc == 0 else []))
                    qcev.append(S.op("v", lambda e, j=j, slot_=slot_: e.tensor_scalar(
                        Qcb[:, j * 512:(j + 1) * 512],
                        pk[:, slot_ * 512:(slot_ + 1) * 512],
                        SCALE1, None, ALU.mult), deps=[mms[-1]]))
                    last_mm = mms[-1]
                use_w("A", last_mm)
                qrev = []
                last_qr_mm = None
                for ti in range(4):
                    i = bi * 4 + ti
                    par = i % 2
                    mms = []
                    for dc in range(8):
                        mms.append(S.op("t", lambda e, i=i, dc=dc: e.matmul(
                            pq[:, 0:512],
                            sb(xnT, dc * T + i * 128, [[8 * T, 128], [1, 128]]),
                            wbufB[:, dc * 512:(dc + 1) * 512],
                            start=(dc == 0), stop=(dc == 7)),
                            deps=[wqr, pv["qm"], pv["yd"], pv["cv"],
                                  qrev[-1] if qrev else pv["qr"]]
                            if dc == 0 else []))
                    last_qr_mm = mms[-1]
                    t1 = S.op("y", lambda e, i=i, par=par: e.dma_start(
                        rqc[:, par * 512:(par + 1) * 512],
                        ropeqc_t[i * 128:(i + 1) * 128, :]),
                        deps=[qrev[-2] if len(qrev) >= 2 else pv["qr"]])
                    t2 = S.op("y", lambda e, i=i, par=par: e.dma_start(
                        rqs[:, par * 512:(par + 1) * 512],
                        ropeqs_t[i * 128:(i + 1) * 128, :]))
                    ro1 = S.op("v", lambda e, par=par: e.tensor_tensor(
                        sb(miscf, 128, [[1024, 128], [1, 512]]),
                        pq[:, 0:512],
                        sb(rqc, par * 512, [[1024, 128], [1, 512]]), ALU.mult),
                        deps=[mms[-1], t1, t2])
                    ro2 = S.op("v", lambda e, par=par: e.tensor_tensor(
                        sb(miscf, 640, [[1024, 128], [16, 16], [1, 16]]),
                        sb(pq, 16, [[1024, 128], [32, 16], [1, 16]]),
                        sb(rqs, par * 512, [[1024, 128], [32, 16], [1, 16]]),
                        ALU.mult))
                    ro3 = S.op("v", lambda e: e.scalar_tensor_tensor(
                        sb(miscb, 512, [[1024, 128], [32, 16], [1, 16]]),
                        sb(miscf, 640, [[1024, 128], [16, 16], [1, 16]]),
                        1.0,
                        sb(miscf, 128, [[1024, 128], [32, 16], [1, 16]]),
                        ALU.mult, ALU.add), deps=[ro1, ro2])
                    ro4 = S.op("v", lambda e, par=par: e.tensor_tensor(
                        sb(miscf, 640, [[1024, 128], [16, 16], [1, 16]]),
                        sb(pq, 0, [[1024, 128], [32, 16], [1, 16]]),
                        sb(rqs, par * 512 + 16, [[1024, 128], [32, 16], [1, 16]]),
                        ALU.mult), deps=[ro3])
                    ro5 = S.op("v", lambda e: e.scalar_tensor_tensor(
                        sb(miscb, 512 + 16, [[1024, 128], [32, 16], [1, 16]]),
                        sb(miscf, 640, [[1024, 128], [16, 16], [1, 16]]),
                        1.0,
                        sb(miscf, 128 + 16, [[1024, 128], [32, 16], [1, 16]]),
                        ALU.mult, ALU.add), deps=[ro4])
                    trs = []
                    for gq in range(4):
                        trs.append(S.op("t", lambda e, gq=gq: e.transpose(
                            pT[:, gq * 128:(gq + 1) * 128],
                            qrope_bf[:, gq * 128:(gq + 1) * 128],
                            identbf[:, :]),
                            deps=[ro5, qrev[-1] if qrev else None]
                            if gq == 0 else []))
                    for gq in range(3):
                        S.op("v", lambda e, gq=gq, ti=ti: e.tensor_copy(
                            Qrb[:, gq * 512 + ti * 128:gq * 512 + ti * 128 + 128],
                            pT[:, gq * 128:(gq + 1) * 128]),
                            deps=trs if gq == 0 else [])
                    qrev.append(S.op("v", lambda e, ti=ti: e.tensor_copy(
                        Qrb[:, 3 * 512 + ti * 128:3 * 512 + ti * 128 + 128],
                        pT[:, 3 * 128:4 * 128])))
                y_done = []
                qmax_prev = pv["qm"]
                nrm_prev = pv["nr"]
                em_prev = pv["em"]
                exp_prev = [pv["e0"], pv["e1"]]
                pv_prev = pv["pv"]
                for h in range(H):
                    hp = h % 2
                    j = h // 2
                    qrow = QR_ROW[h % 4]
                    red = []
                    prev_red = qmax_prev
                    first_mm = True
                    for st_ in range(4):
                        kmax = bi * 512 + (st_ + 1) * 128
                        npc = (kmax + 1023) // 1024
                        for piece in range(npc):
                            plo = piece * 1024
                            phi = min(kmax, plo + 1024)
                            for kg in range((phi - plo + 511) // 512):
                                klo = plo + kg * 512
                                khi = min(phi, klo + 512)
                                d1 = [qcev[-1], qrev[-1], prev_red,
                                      em_prev] if first_mm else [prev_red]
                                first_mm = False
                                S.op("t", lambda e, st_=st_, klo=klo, khi=khi,
                                     plo=plo, hp=hp, j=j, bi=bi: e.matmul(
                                         pq[:, klo - plo:khi - plo],
                                         Qcb[hp * 64:hp * 64 + 64,
                                             j * 512 + st_ * 128:j * 512 + (st_ + 1) * 128],
                                         KT[hp * 64:hp * 64 + 64,
                                            j * T + klo:j * T + khi],
                                         start=True, stop=False), deps=d1)
                                mm2_ = S.op("t", lambda e, st_=st_, klo=klo, khi=khi,
                                            plo=plo, qrow=qrow, h=h: e.matmul(
                                                pq[:, klo - plo:khi - plo],
                                                Qrb[qrow:qrow + 32,
                                                    (h // 4) * 512 + st_ * 128:
                                                    (h // 4) * 512 + (st_ + 1) * 128],
                                                KRr[qrow:qrow + 32, klo:khi],
                                                start=False, stop=True,
                                                tile_position=(qrow, 0)))
                            dep_m = [mm2_]
                            if plo <= kmax - 128 < phi:
                                dep_m = [S.op("v", lambda e, kmax=kmax, plo=plo:
                                              e.tensor_tensor(
                                                  pq[:, kmax - 128 - plo:kmax - plo],
                                                  pq[:, kmax - 128 - plo:kmax - plo],
                                                  maskq[:, :], ALU.add),
                                              deps=[mm2_])]
                            rd = S.op("v", lambda e, st_=st_, plo=plo, phi=phi,
                                      first=(piece == 0): e.tensor_reduce(
                                          mq4[:, st_:st_ + 1] if first
                                          else tm4[:, st_:st_ + 1],
                                          pq[:, 0:phi - plo], axis=AX.X,
                                          op=ALU.max),
                                      deps=dep_m + ([em_prev] if st_ == 0 and piece == 0 else []))
                            if piece > 0:
                                rd = S.op("v", lambda e, st_=st_: e.tensor_tensor(
                                    mq4[:, st_:st_ + 1], mq4[:, st_:st_ + 1],
                                    tm4[:, st_:st_ + 1], ALU.max), deps=[rd])
                            red.append(rd)
                            prev_red = rd
                    qmax_prev = red[-1]
                    pv_mms = []
                    for c in range(ncnk):
                        ps_ = c % 2
                        q0 = max(0, c * 128 - bi * 512)
                        mm1 = S.op("t", lambda e, c=c, q0=q0, ps_=ps_, hp=hp, j=j:
                                   e.matmul(
                                       pk[:, ps_ * 1024 + q0:ps_ * 1024 + 512],
                                       KT[hp * 64:hp * 64 + 64,
                                          j * T + c * 128:j * T + (c + 1) * 128],
                                       Qcb[hp * 64:hp * 64 + 64,
                                           j * 512 + q0:(j + 1) * 512],
                                       start=True, stop=False),
                                   deps=[exp_prev[ps_], qmax_prev,
                                         y_done[-1] if y_done else pv["yd"]]
                                   if c < 2 else [exp_prev[ps_]])
                        mm2_ = S.op("t", lambda e, c=c, q0=q0, ps_=ps_, qrow=qrow,
                                    h=h: e.matmul(
                                        pk[:, ps_ * 1024 + q0:ps_ * 1024 + 512],
                                        KRr[qrow:qrow + 32, c * 128:(c + 1) * 128],
                                        Qrb[qrow:qrow + 32,
                                            (h // 4) * 512 + q0:(h // 4 + 1) * 512],
                                        start=False, stop=True,
                                        tile_position=(qrow, 0)))
                        dep_mask = [mm2_]
                        if c >= 4 * bi:
                            dep_mask = [S.op("v", lambda e, q0=q0, ps_=ps_:
                                             e.tensor_tensor(
                                                 pk[:, ps_ * 1024 + q0:ps_ * 1024 + q0 + 128],
                                                 pk[:, ps_ * 1024 + q0:ps_ * 1024 + q0 + 128],
                                                 maskk[:, :], ALU.add),
                                             deps=[mm2_])]
                        ex = S.op("s", lambda e, q0=q0, ps_=ps_: e.activation(
                            PTr[:, ps_ * 1024 + q0:ps_ * 1024 + 512],
                            pk[:, ps_ * 1024 + q0:ps_ * 1024 + 512], AF.Exp),
                            deps=dep_mask + [pv_prev])
                        exp_prev[ps_] = ex
                        pv_prev = S.op("t", lambda e, c=c, q0=q0, ps_=ps_, h=h,
                                       ncnk=ncnk: e.matmul(
                                           pY[0:65, q0:512],
                                           Vsb[:, c * VS + h * 65:c * VS + (h + 1) * 65],
                                           PTr[:, ps_ * 1024 + q0:ps_ * 1024 + 512],
                                           start=(c == 0), stop=(c == ncnk - 1)),
                                       deps=[ex, y_done[-1] if y_done else pv["yd"]]
                                       if c == 0 else [ex])
                        pv_mms.append(pv_prev)
                    dre = S.op("v", lambda e: e.tensor_copy(denrow, pY[64:65, 0:512]),
                               deps=[pv_mms[-1], nrm_prev])
                    dq = []
                    for st_ in range(4):
                        dq.append(S.op("t", lambda e, st_=st_: e.matmul(
                            pk[:, st_:st_ + 1],
                            denrow[:, st_ * 128:(st_ + 1) * 128], ones11[:, :],
                            start=True, stop=True),
                            deps=[dre, exp_prev[0], exp_prev[1]]
                            if st_ == 0 else []))
                    em = S.op("s", lambda e: e.activation(emq4, mq4, AF.Exp),
                              deps=[red[-1], nrm_prev])
                    em_prev = em
                    S.op("v", lambda e, h=h: e.scalar_tensor_tensor(
                        nrq4, emq4, cexp_sb[:, h:h + 1], pk[:, 0:4],
                        ALU.mult, ALU.add), deps=[em, dq[-1], cexp_done])
                    nr2 = S.op("v", lambda e: e.reciprocal(nrq4, nrq4),
                               deps=[S.ops["v"][-1]])
                    nrow = []
                    for st_ in range(4):
                        nrow.append(S.op("t", lambda e, st_=st_: e.matmul(
                            pk[0:1, 512 + st_ * 128:512 + (st_ + 1) * 128],
                            nrq4[:, st_:st_ + 1], identf[:, :],
                            start=True, stop=True),
                            deps=[nr2] if st_ == 0 else []))
                    nre = S.op("v", lambda e: e.tensor_copy(
                        nrmrow_t[:, :], pk[0:1, 512:1024]), deps=[nrow[-1]])
                    nbc = S.op("t", lambda e: e.matmul(
                        pk[0:64, 1024:1536], onesf[:, :], nrmrow_t[:, :],
                        start=True, stop=True), deps=[nre])
                    nbe = S.op("v", lambda e: e.tensor_copy(
                        nrmbc_sb[:, :], pk[0:64, 1024:1536]), deps=[nbc])
                    nrm_prev = nbe
                    if hp == 0:
                        yd = S.op("v", lambda e, j=j: e.tensor_tensor(
                            Ysb[0:64, j * 512:(j + 1) * 512],
                            pY[0:64, 0:512], nrmbc_sb[:, :], ALU.mult),
                            deps=[nbe])
                    else:
                        yst = S.op("v", lambda e: e.tensor_tensor(
                            ystage[:, :], pY[0:64, 0:512], nrmbc_sb[:, :],
                            ALU.mult),
                            deps=[nbe, y_done[-2] if len(y_done) >= 2
                                  else pv["yd2"]])
                        yd = S.op("g", lambda e, j=j: e.dma_start(
                            Ysb[64:128, j * 512:(j + 1) * 512], ystage[:, :]),
                            deps=[yst])
                    y_done.append(yd)
                wo_last = pv["wo"]
                for st_ in range(4):
                    slot_g = gslot(s, bi * 4 + st_)
                    xap = sb(x_sb, (st_ % 2) * D, [[2 * D, 128], [1, D]])
                    xl = S.op("y", lambda e, slot_g=slot_g, xap=xap: e.dma_start(
                        xap, xw[slot_g * 128:(slot_g + 1) * 128, :]),
                        deps=[last_store[slot_g], st_par[st_ % 2],
                              cv_marks[-2] if len(cv_marks) >= 2 else None])
                    mms = []
                    for jc in range(8):
                        for g in range(2):
                            mms.append(S.op("t", lambda e, st_=st_, jc=jc, g=g:
                                            e.matmul(
                                                pq[:, g * 512:(g + 1) * 512],
                                                Ysb[:, jc * 512 + st_ * 128:
                                                    jc * 512 + (st_ + 1) * 128],
                                                wbufB[:, 4096 + jc * 1024 + g * 512:
                                                      4096 + jc * 1024 + (g + 1) * 512],
                                                start=(jc == 0), stop=(jc == 7)),
                                            deps=[wo, y_done[-1], y_done[-2],
                                                  wo_last, qmax_prev,
                                                  cv_marks[-1] if cv_marks
                                                  else pv["cv"]]
                                            if jc == 0 and g == 0 else []))
                    wo_last = mms[-1]
                    add = S.op("v", lambda e, xap=xap: e.tensor_tensor(
                        xap, xap, pq[:, 0:1024], ALU.add), deps=[xl, wo_last])
                    cv_marks.append(add)
                    last_store[slot_g] = st_par[st_ % 2] = S.op(
                        "y", lambda e, slot_g=slot_g, xap=xap: e.dma_start(
                            xdst[slot_g * 128:(slot_g + 1) * 128, :], xap),
                        deps=[add])
                use_w("B", wo_last)
                pv = {"qc": qcev[-1], "qr": qrev[-1], "pv": pv_prev,
                      "wo": wo_last, "yd": y_done[-1], "yd2": y_done[-2],
                      "qm": qmax_prev, "nr": nrm_prev, "em": em_prev,
                      "e0": exp_prev[0], "e1": exp_prev[1],
                      "cv": cv_marks[-1]}
            mark["C_T"] = [wo_last, last_qr_mm]
            mark["C_V"] = [cv_marks[-1]] if cv_marks else []

    step = 0
    for s in range(SPC):
        for layer in range(2):
            if step >= upto:
                break
            with ExitStack() as actx:
                x_sb = actx.enter_context(
                    nc.sbuf_tensor(f"x_sb_{s}_{layer}", [128, 2 * D], F32))
                if layer == 0:
                    w_ref = S.op("g", lambda e: e.dma_start(
                        wbufB[:, 0:6144], wfin_hbm[:, :]),
                        deps=unfold_done + wstate["B"]["last"] + tld)
                    wstate["B"]["last"] = []
                    with ExitStack() as sqctx:
                        sq_holder["t"] = sqctx.enter_context(
                            nc.sbuf_tensor(f"sq_sb_{s}_{layer}", [128, 1024], F32))
                        emit_A(s, 0, x_sb, w_ref)
                    step += 1
                    if step < upto:
                        emit_B0(s)
                        step += 1
                    if step < upto:
                        emit_C0(s, x_sb)
                        step += 1
                else:
                    w_ref = load_w("A", basewt_t, 1024)
                    with ExitStack() as sqctx:
                        sq_holder["t"] = sqctx.enter_context(
                            nc.sbuf_tensor(f"sq_sb_{s}_{layer}", [128, 1024], F32))
                        emit_A(s, 1, x_sb, w_ref)
                    use_w("A", S.ops["t"][-1])
                    step += 1
                    if step < upto:
                        emit_B1(s)
                        step += 1
                    if step < upto:
                        emit_C1(s, x_sb, final=True)
                        step += 1
        else:
            continue
        break
    if upto < 12:
        if debug:
            S.op("g", lambda e: e.dma_start(dln[:, :], miscf[:, 100:104]),
                 deps=[S.ops["v"][-1]])
            S.op("g", lambda e: e.dma_start(dg1r[:, :], g1r_t[:, :]))
            S.op("g", lambda e: e.dma_start(dbias[:, :], biasr_t[:, :]))
            if "dbgextra" in _SKIP:
                S.finalize(ctx); ctx.close(); return nc
            S.op("g", lambda e: e.dma_start(dbgw[:, :], wfin_hbm[:, :]),
                 deps=[S.ops["t"][-1] if S.ops["t"] else None,
                       S.ops["v"][-1] if S.ops["v"] else None] + unfold_done)
            if upto != 0:
                S.op("g", lambda e: e.dma_start(dbgx[:, :], xnT[:, :]))
                S.op("g", lambda e: e.dma_start(dbgk[:, :], KT[:, :]))
                S.op("g", lambda e: e.dma_start(dbgv[:, :], Vsb[:, 0:NT * VS]))
                S.op("g", lambda e: e.dma_start(dbgr[:, :], KRr[:, :]))
        if "outcopy" not in _SKIP:
            for sl in range(SPC * NT):
                S.op("y", lambda e, sl=sl: e.dma_start(
                    out_t[sl * 128:(sl + 1) * 128, :],
                    xw[sl * 128:(sl + 1) * 128, :]),
                    deps=[last_store[sl], S.ops["v"][-1]])

    if verbose:
        tot = {e: len(S.ops[e]) for e in S.ENGS}
        print("ops:", tot, "total:", sum(tot.values()))
        print("sbuf bytes remaining:", nc.sbuf_bytes_remaining)

    S.finalize(ctx)
    ctx.close()
    return nc


def _host_tables():
    ang = np.arange(T, dtype=np.float64)[:, None] * (
        1.0 / 10000.0 ** (np.arange(16, dtype=np.float64) / 16.0))[None, :]
    cos = np.cos(ang).astype(np.float32)
    sin = np.sin(ang).astype(np.float32)
    ropekc = np.concatenate([cos, cos], axis=1)
    ropeks = np.concatenate([-sin, sin], axis=1)
    qc1 = (np.tile(ropekc, (1, H)) * np.float32(SCALE1)).astype(np.float32)
    qs1 = (np.tile(ropeks, (1, H)) * np.float32(SCALE1)).astype(np.float32)
    iot = np.arange(128)
    maskq = np.where(iot[None, :] <= iot[:, None], 0.0, NEGINF).astype(np.float32)
    maskk = np.ascontiguousarray(maskq.T)
    return qc1, qs1, ropekc.astype(np.float32), ropeks.astype(np.float32), maskq, maskk


_NC_CACHE = {}


def kernel(**inputs):
    gi = lambda n: np.asarray(inputs[n])
    x = np.ascontiguousarray(gi("L0_attn_in_features").astype(np.float32))

    qc1, qs1, kc1, ks1, maskq, maskk = _host_tables()

    wqr = gi("L1_attn_Wqr").astype(np.float32).reshape(D, H, DR)
    perm = [4 * g + j for g in range(4) for j in QR_PERM]
    wqr_p = np.ascontiguousarray(wqr[:, perm, :].reshape(D, H * DR))

    common = {
        "l0wq": np.ascontiguousarray(gi("L0_attn_Wq").astype(np.float32).T),
        "l0wk": np.ascontiguousarray(gi("L0_attn_Wk").astype(np.float32).T),
        "l0wv": np.ascontiguousarray(gi("L0_attn_Wv").astype(np.float32).T),
        "l0wo": np.ascontiguousarray(gi("L0_attn_Wo").astype(np.float32).T),
        "g1r": np.ascontiguousarray(
            gi("L0_ffn_G1_R").astype(np.float32)[0].transpose(1, 0, 2).reshape(4, 1024)),
        "g2r": np.ascontiguousarray(
            gi("L0_ffn_G2_R").astype(np.float32)[:, :, 0, :].transpose(0, 2, 1).reshape(4, 256)),
        "g1i": np.ascontiguousarray(
            gi("L0_ffn_G1_I").astype(np.float32)[0].transpose(1, 0, 2).reshape(4, 1024)),
        "g2i": np.ascontiguousarray(
            gi("L0_ffn_G2_I").astype(np.float32)[:, :, 0, :].transpose(0, 2, 1).reshape(4, 256)),
        "biasr": np.ascontiguousarray(
            np.tile(gi("L0_ffn_bias_real").astype(np.float32)[None, :], (128, 1))),
        "biasi": np.ascontiguousarray(
            np.tile(gi("L0_ffn_bias_imag").astype(np.float32)[None, :], (128, 1))),
        "l1wq": np.ascontiguousarray(gi("L1_attn_Wq").astype(np.float32)),
        "l1wqr": wqr_p,
        "l1wkr": np.ascontiguousarray(gi("L1_attn_Wkr").astype(np.float32)),
        "l1wkvd": np.ascontiguousarray(gi("L1_attn_Wkvd").astype(np.float32)),
        "l1wku": np.ascontiguousarray(gi("L1_attn_Wku").astype(np.float32)),
        "l1wvu": np.ascontiguousarray(gi("L1_attn_Wvu").astype(np.float32)),
        "l1wo": np.ascontiguousarray(gi("L1_attn_Wo").astype(np.float32)),
        "lobo": np.ascontiguousarray(
            np.tile(gi("L1_attn_lobo_log").astype(np.float32)[None, :], (128, 1))),
        "basewt": np.ascontiguousarray(gi("L1_ffn_base_W").astype(np.float32).T),
        "basebbc": np.ascontiguousarray(
            np.tile(gi("L1_ffn_base_b").astype(np.float32)[None, :], (128, 1))),
        "sharedat": np.ascontiguousarray(gi("L1_ffn_shared_A").astype(np.float32).T),
        "sharedbt": np.ascontiguousarray(gi("L1_ffn_shared_B").astype(np.float32).T),
        "routerwt": np.ascontiguousarray(gi("L1_ffn_router_W").astype(np.float32).T),
        "routerbbc": np.ascontiguousarray(
            np.tile(gi("L1_ffn_router_b").astype(np.float32)[None, :], (128, 1))),
        "routedat": np.ascontiguousarray(
            gi("L1_ffn_routed_A").astype(np.float32).transpose(2, 0, 1).reshape(D, NE * R)),
        "routedbt": np.ascontiguousarray(
            gi("L1_ffn_routed_B").astype(np.float32).transpose(0, 2, 1).reshape(NE * R, D)),
        "identbf": np.eye(128, dtype=ml_dtypes.bfloat16),
        "identf": np.eye(128, dtype=np.float32),
        "maskq": maskq,
        "maskk": maskk,
        "ropeqc": qc1,
        "ropeqs": qs1,
        "ropekc": kc1,
        "ropeks": ks1,
    }

    nc = build_nc()
    in_maps = []
    for c in range(NCORES):
        m = dict(common)
        m["x"] = np.ascontiguousarray(x[c * SPC:(c + 1) * SPC].reshape(SPC * T, D))
        in_maps.append(m)
    res = run_bass_kernel_spmd(nc, in_maps, core_ids=list(range(NCORES)))
    outs = [np.asarray(res.results[c]["out"]).reshape(SPC, T, D)
            for c in range(NCORES)]
    return np.concatenate(outs, axis=0).astype(np.float32)


if __name__ == "__main__":
    build_nc(verbose=True)
